# version 18 (final)
# speedup vs baseline: 1.2417x; 1.0165x over previous
import numpy as np
from contextlib import ExitStack

import concourse.bass as bass
import concourse.mybir as mybir
from concourse.bass_utils import run_bass_kernel_spmd

F32 = mybir.dt.float32
I32 = mybir.dt.int32
AF = mybir.ActivationFunctionType
ALU = mybir.AluOpType
AX = mybir.AxisListType

B, S, D = 4, 4096, 1024
II = 2816
TOK = 2048
NS = 16
KB = 256
NT_ALL = (B * S) // 128
NT_SEQ = S // 128
NIC = II // 128
EPS = 1e-5
QMAX = 127.0
G0 = float(2.0 ** 17)
G0R = float(2.0 ** -17)
G1 = float(2.0 ** 23)
G1R = float(2.0 ** -40)

ENGINES = ("sync", "tensor", "vector", "scalar", "gpsimd")
DEBUG_L1 = False


class Prog:

    DEF_COST = {"sync": 80, "tensor": 700, "vector": 400,
                "scalar": 450, "gpsimd": 120}
    TINY = {"smv", "gm1", "hm1", "e0b", "sxb", "rsxb", "gacc", "hmacc",
            "hmout", "epsb", "clipb", "hmv", "hmg", "rs0", "s1b", "s2b",
            "sdb"}

    def __init__(self):
        self.ops = []
        self.last_w = {}
        self.readers = {}
        self.defc = None
        self.boost = 0.0

    def op(self, engine, fn, reads=(), writes=(), deps=(), dma=False,
           cost=None):
        i = len(self.ops)
        dep = set(d for d in deps if d is not None)
        for k in reads:
            w = self.last_w.get(k)
            if w is not None:
                dep.add(w)
        for k in writes:
            w = self.last_w.get(k)
            if w is not None:
                dep.add(w)
            dep.update(self.readers.get(k, ()))
        for k in reads:
            self.readers.setdefault(k, []).append(i)
        for k in writes:
            self.last_w[k] = i
            self.readers[k] = []
        if cost is None:
            if dma:
                cost = 1400
            elif (reads or writes) and all(
                    k[0] in self.TINY for k in list(reads) + list(writes)):
                cost = 100
            elif engine == "vector" and self.defc is not None:
                cost = self.defc
            else:
                cost = self.DEF_COST[engine]
        self.ops.append(dict(engine=engine, fn=fn, deps=dep, dma=dma,
                             cost=cost, boost=self.boost))
        return i

    def schedule(self):
        import heapq
        n = len(self.ops)
        succ = [[] for _ in range(n)]
        indeg = [0] * n
        for i, o in enumerate(self.ops):
            for d in o["deps"]:
                succ[d].append(i)
                indeg[i] += 1
        prio = [0.0] * n
        for i in range(n - 1, -1, -1):
            m = 0.0
            for s in succ[i]:
                if prio[s] > m:
                    m = prio[s]
            prio[i] = self.ops[i]["cost"] + self.ops[i]["boost"] + m
        QT = 25000.0
        eng_free = {e: 0.0 for e in ENGINES}
        fin = [0.0] * n
        rt = [0.0] * n
        heaps = {e: [] for e in ENGINES}

        def push(i):
            e = self.ops[i]["engine"]
            q = int(max(rt[i], eng_free[e]) // QT)
            heapq.heappush(heaps[e], (q, -prio[i], i))

        for i in range(n):
            if indeg[i] == 0:
                push(i)
        order = []
        while True:
            best = None
            for e in ENGINES:
                h = heaps[e]
                while h:
                    q, np_, i = h[0]
                    qe = int(max(rt[i], eng_free[e]) // QT)
                    if qe > q:
                        heapq.heapreplace(h, (qe, np_, i))
                    else:
                        break
                if h:
                    q, np_, i = h[0]
                    st = max(rt[i], eng_free[e])
                    if best is None or (st, np_) < (best[0], best[1]):
                        best = (st, np_, e)
            if best is None:
                break
            st, _, e = best
            _, _, i = heapq.heappop(heaps[e])
            o = self.ops[i]
            if o["dma"]:
                eng_free[e] = st + 100.0
                fin[i] = st + o["cost"] + 600.0
            else:
                eng_free[e] = st + o["cost"]
                fin[i] = eng_free[e]
            order.append(i)
            for s in succ[i]:
                indeg[s] -= 1
                if fin[i] > rt[s]:
                    rt[s] = fin[i]
                if indeg[s] == 0:
                    push(s)
        assert len(order) == n, "cycle in op DAG"
        import os
        if os.environ.get("SCHED_DEBUG"):
            mk = max(fin) / 1000.0
            busy = {e: 0.0 for e in ENGINES}
            for o in self.ops:
                if not o["dma"]:
                    busy[o["engine"]] += o["cost"] / 1000.0
            ft = min((fin[i] - self.ops[i]["cost"] for i in range(n)
                      if self.ops[i]["engine"] == "tensor"), default=0)
            print(f"SCHED: n={n} makespan={mk:.0f}us "
                  f"busy={ {e: round(b) for e, b in busy.items()} } "
                  f"first_tensor={ft/1000.0:.0f}us")
        inv = [0] * n
        for j, i in enumerate(order):
            inv[i] = j
        self.ops = [dict(self.ops[i], deps=set(inv[d]
                         for d in self.ops[i]["deps"]))
                    for i in order]

    RAW_DIST = 10**9

    DMA_LANES = 6

    def emit_into(self, block, sems, dsems):
        self.schedule()
        eng_stream = {e: [] for e in ENGINES}
        pos = {}
        for i, o in enumerate(self.ops):
            pos[i] = len(eng_stream[o["engine"]])
            eng_stream[o["engine"]].append(i)

        def dep_needs_sync(d, i):
            de, e = self.ops[d]["engine"], self.ops[i]["engine"]
            if de != e:
                return True
            if self.ops[d]["dma"]:
                return True
            return (pos[i] - pos[d]) <= self.RAW_DIST

        needed = set()
        for i, o in enumerate(self.ops):
            if o["dma"]:
                needed.add(i)
            for d in o["deps"]:
                if dep_needs_sync(d, i):
                    needed.add(d)
        lane = {}
        lane_cnt = {}
        inc_at = {}
        for e in ENGINES:
            c = 0
            nd = 0
            for i in eng_stream[e]:
                if self.ops[i]["dma"]:
                    ln = nd % self.DMA_LANES
                    lane[i] = ln
                    lane_cnt[(e, ln)] = lane_cnt.get((e, ln), 0) + 16
                    inc_at[i] = lane_cnt[(e, ln)]
                    nd += 1
                else:
                    if i in needed:
                        c += 1
                    inc_at[i] = c

        def run_engine(e, eng):
            hwm = {}
            for i in eng_stream[e]:
                o = self.ops[i]
                for d in sorted(o["deps"]):
                    if not dep_needs_sync(d, i):
                        continue
                    de = self.ops[d]["engine"]
                    if self.ops[d]["dma"]:
                        key = (de, "dma", lane[d])
                        sem = dsems[de][lane[d]]
                    else:
                        key = (de, "c")
                        sem = sems[de]
                    tgt = inc_at[d]
                    if tgt > hwm.get(key, 0):
                        eng.wait_ge(sem, tgt)
                        hwm[key] = tgt
                if o["dma"]:
                    key = (e, "dma", lane[i])
                    prev = inc_at[i] - 16
                    if prev > 0 and prev > hwm.get(key, 0):
                        eng.wait_ge(dsems[e][lane[i]], prev)
                        hwm[key] = prev
                inst = o["fn"](eng)
                if o["dma"]:
                    inst.then_inc(dsems[e][lane[i]], 16)
                elif i in needed:
                    if inst is None or e == "gpsimd":
                        eng.sem_inc(sems[e], 1)
                    else:
                        inst.then_inc(sems[e], 1)

        for e in ENGINES:
            getattr(block, e)(lambda eng, e=e: run_engine(e, eng))
        tot_c = {e: 0 for e in ENGINES}
        for e in ENGINES:
            for i in eng_stream[e]:
                if not self.ops[i]["dma"] and i in needed:
                    tot_c[e] += 1

        def fin(eng):
            for e in ENGINES:
                if tot_c[e]:
                    eng.wait_ge(sems[e], tot_c[e])
            for (e, ln), v in sorted(lane_cnt.items()):
                eng.wait_ge(dsems[e][ln], v)
        block.gpsimd(fin)


def K(name, lo, hi):
    return [(name, j) for j in range(lo // 128, (hi + 127) // 128)]


def K1(name):
    return [(name, 0)]



def build_l1():
    nc = bass.Bass("TRN2", target_bir_lowering=False)
    ap = lambda *a, **k: nc.dram_tensor(*a, **k).ap()

    Xstat = ap("Xstat", [B * S, D], F32, kind="ExternalInput")
    Xown = ap("Xown", [TOK, D], F32, kind="ExternalInput")
    Wq = ap("Wq", [D, D], F32, kind="ExternalInput")
    Wk = ap("Wk", [D, D], F32, kind="ExternalInput")
    Wv = ap("Wv", [D, D], F32, kind="ExternalInput")
    bqv = ap("bqv", [128, 8], F32, kind="ExternalInput")
    bkv = ap("bkv", [128, 8], F32, kind="ExternalInput")
    bvv = ap("bvv", [128, 8], F32, kind="ExternalInput")
    Wg = ap("Wg", [II, D], F32, kind="ExternalInput")
    Wu = ap("Wu", [II, D], F32, kind="ExternalInput")
    clipx = ap("clipx", [1, 1], F32, kind="ExternalInput")
    masksI = ap("masks", [2 * 128, KB], F32, kind="ExternalInput")
    ident = ap("ident", [128, 128], F32, kind="ExternalInput")

    acc_o = ap("acc", [TOK, D], F32, kind="ExternalOutput")
    acc1_o = ap("acc1", [TOK, D], F32, kind="ExternalOutput")
    hi0_o = ap("hi0", [II, TOK], F32, kind="ExternalOutput")
    hi1_o = ap("hi1", [II, TOK], F32, kind="ExternalOutput")
    hm_o = ap("hm", [1, 8], F32, kind="ExternalOutput")

    p = Prog()
    with ExitStack() as es:
        E = es.enter_context
        xb0 = E(nc.sbuf_tensor("xb0", [128, D], F32))
        xb1 = E(nc.sbuf_tensor("xb1", [128, D], F32))
        ab0 = E(nc.sbuf_tensor("ab0", [128, D], F32))
        ab1 = E(nc.sbuf_tensor("ab1", [128, D], F32))
        xnF = E(nc.sbuf_tensor("xnF", [128, 8 * TOK], F32))
        A1 = E(nc.sbuf_tensor("A1", [128, 8 * TOK], F32))
        A2 = E(nc.sbuf_tensor("A2", [128, 8192], F32))
        A3 = E(nc.sbuf_tensor("A3", [128, 4096], F32))
        A4 = E(nc.sbuf_tensor("A4", [128, 1024], I32))
        A5 = E(nc.sbuf_tensor("A5", [128, 1024], F32))
        wtp = E(nc.sbuf_tensor("wtp", [128, 256], I32))
        statsM = E(nc.sbuf_tensor("statsM", [128, NT_SEQ], F32))
        statsR = E(nc.sbuf_tensor("statsR", [128, NT_SEQ], F32))
        smv = E(nc.sbuf_tensor("smv", [128, 46], F32))
        gacc = E(nc.sbuf_tensor("gacc", [128, 2], F32))
        hmacc = E(nc.sbuf_tensor("hmacc", [128, 2], F32))
        epsb = E(nc.sbuf_tensor("epsb", [128, 1], F32))
        onesb = E(nc.sbuf_tensor("onesb", [128, 128], F32))
        e0b = E(nc.sbuf_tensor("e0b", [128, 1], F32))
        sxb = E(nc.sbuf_tensor("sxb", [128, 1], F32))
        rsxb = E(nc.sbuf_tensor("rsxb", [128, 1], F32))
        gm1 = E(nc.sbuf_tensor("gm1", [1, 1], F32))
        hm1 = E(nc.sbuf_tensor("hm1", [1, 1], F32))
        idt = E(nc.sbuf_tensor("idt", [128, 128], F32))
        bqb = E(nc.sbuf_tensor("bqb", [128, 8], F32))
        bkb = E(nc.sbuf_tensor("bkb", [128, 8], F32))
        bvb = E(nc.sbuf_tensor("bvb", [128, 8], F32))
        maskb = E(nc.sbuf_tensor("maskb", [128, 2 * KB], F32))
        negM = E(nc.sbuf_tensor("negM", [128, NS], F32))
        Db = E(nc.sbuf_tensor("Db", [128, NS], F32))
        rb = E(nc.sbuf_tensor("rb", [128, NS], F32))
        e0s = E(nc.sbuf_tensor("e0s", [128, NS], F32))
        expb = E(nc.sbuf_tensor("expb", [128, KB], F32))
        sSb = E(nc.sbuf_tensor("sSb", [128, KB], F32))
        clipb = E(nc.sbuf_tensor("clipb", [1, 1], F32))
        hmout = E(nc.sbuf_tensor("hmout", [1, 8], F32))
        colz = E(nc.sbuf_tensor("colz", [128, 128], F32))
        pA = E(nc.psum_tensor("pA", [128, 512], F32))
        pB = E(nc.psum_tensor("pB", [128, 512], F32))
        pC = E(nc.psum_tensor("pC", [128, 512], F32))
        pD = E(nc.psum_tensor("pD", [128, 512], F32))
        pT0 = E(nc.psum_tensor("pT0", [128, 512], F32))
        pT1 = E(nc.psum_tensor("pT1", [128, 512], F32))
        pT2 = E(nc.psum_tensor("pT2", [128, 512], F32))
        pT3 = E(nc.psum_tensor("pT3", [128, 512], F32))
        sems = {e: E(nc.semaphore("s_" + e)) for e in ENGINES}
        dsems = {e: [E(nc.semaphore(f"d_{e}{k}")) for k in range(Prog.DMA_LANES)]
                 for e in ("sync", "scalar", "gpsimd")}
        block = E(nc.Block())

        PS = {id(pA): "pA", id(pB): "pB", id(pC): "pC", id(pD): "pD",
              id(pT0): "pT0", id(pT1): "pT1", id(pT2): "pT2", id(pT3): "pT3"}
        psk = lambda t: [(PS[id(t)], 0)]
        xk = ("xk0", "xk1")
        akey = ("ak0", "ak1")
        xbs = (xb0, xb1)
        abs_ = (ab0, ab1)


        def sc(col):
            return [("smv", col)]

        p.op("sync", lambda e: e.dma_start(out=idt[:], in_=ident),
             writes=K1("idt"), dma=True)
        p.op("sync", lambda e: e.dma_start(out=bqb[:], in_=bqv),
             writes=K1("bqb"), dma=True)
        p.op("sync", lambda e: e.dma_start(out=bkb[:], in_=bkv),
             writes=K1("bkb"), dma=True)
        p.op("sync", lambda e: e.dma_start(out=bvb[:], in_=bvv),
             writes=K1("bvb"), dma=True)
        p.op("sync", lambda e: e.dma_start(out=clipb[:], in_=clipx),
             writes=K1("clipb"), dma=True)
        p.op("sync", lambda e: e.dma_start(
            out=maskb[:, 0:KB], in_=masksI[0:128, :]),
            writes=K1("maskb"), dma=True)
        p.op("sync", lambda e: e.dma_start(
            out=maskb[:, KB:2 * KB], in_=masksI[128:256, :]),
            writes=K1("maskb"), dma=True)
        p.op("vector", lambda e: e.memset(epsb[:], EPS), writes=K1("epsb"))
        p.op("vector", lambda e: e.memset(onesb[:], 1.0), writes=K1("onesb"))
        p.op("vector", lambda e: e.memset(e0b[:], 0.0), writes=K1("e0b"))
        p.op("vector", lambda e: e.memset(gacc[:], 0.0), writes=[("gacc", 0), ("gacc", 1)])
        p.op("vector", lambda e: e.memset(hmacc[:], 0.0), writes=[("hmacc", 0), ("hmacc", 1)])
        p.op("vector", lambda e: e.memset(hmout[:], 0.0), writes=K1("hmout"))
        p.op("vector", lambda e: e.memset(colz[:], 0.0), writes=K1("colz"))
        p.op("vector", lambda e: e.memset(smv[:], 0.0),
             writes=[("smv", j) for j in range(46)])

        C_S1, C_S2, C_M, C_MM, C_V, C_SQ, C_RS, C_NM, C_AD = \
            0, 2, 4, 6, 8, 10, 12, 14, 16

        def tile_stats(xb, ab, par, store_t=None):
            s1, s2, m, mm, v, sq, rs = (C_S1 + par, C_S2 + par, C_M + par,
                                        C_MM + par, C_V + par, C_SQ + par,
                                        C_RS + par)
            p.op("scalar", lambda e: e.activation(ab[:], xb[:], AF.Square),
                 reads=K1(xk[par]), writes=K1(akey[par]))
            p.op("vector", lambda e: e.reduce_sum(
                smv[:, s1:s1 + 1], xb[:], AX.X),
                reads=K1(xk[par]), writes=sc(s1))
            p.op("vector", lambda e: e.reduce_sum(
                smv[:, s2:s2 + 1], ab[:], AX.X),
                reads=K1(akey[par]), writes=sc(s2))
            p.op("vector", lambda e: e.tensor_scalar(
                smv[:, m:m + 1], smv[:, s1:s1 + 1], 1.0 / D, None, ALU.mult),
                reads=sc(s1), writes=sc(m))
            if store_t is not None:
                p.op("vector", lambda e, t=store_t: e.tensor_copy(
                    statsM[:, t:t + 1], smv[:, m:m + 1]),
                    reads=sc(m), writes=[("statsM", store_t)])
            p.op("vector", lambda e: e.tensor_tensor(
                smv[:, mm:mm + 1], smv[:, m:m + 1], smv[:, m:m + 1], ALU.mult),
                reads=sc(m), writes=sc(mm))
            p.op("vector", lambda e: e.scalar_tensor_tensor(
                smv[:, v:v + 1], smv[:, s2:s2 + 1], 1.0 / D,
                smv[:, mm:mm + 1], ALU.mult, ALU.subtract),
                reads=sc(s2) + sc(mm), writes=sc(v))
            p.op("scalar", lambda e: e.activation(
                smv[:, sq:sq + 1], smv[:, v:v + 1], AF.Sqrt,
                bias=epsb[:], scale=1.0),
                reads=sc(v) + K1("epsb"), writes=sc(sq))
            p.op("vector", lambda e: e.reciprocal(
                smv[:, rs:rs + 1], smv[:, sq:sq + 1]),
                reads=sc(sq), writes=sc(rs))
            if store_t is not None:
                p.op("vector", lambda e, t=store_t: e.tensor_copy(
                    statsR[:, t:t + 1], smv[:, rs:rs + 1]),
                    reads=sc(rs), writes=[("statsR", store_t)])

        if DEBUG_L1:
            sdbg = E(nc.sbuf_tensor("sdbg", [128, 2 * NT_ALL], F32))

        def stat_tile(t):
            par = t % 2
            xb, ab = xbs[par], abs_[par]
            p.op("sync", lambda e, xb=xb, t=t: e.dma_start(
                out=xb[:], in_=Xstat[t * 128:(t + 1) * 128, :]),
                writes=K1(xk[par]), dma=True)
            tile_stats(xb, ab, par, store_t=(t if t < NT_SEQ else None))
            if DEBUG_L1:
                p.op("vector", lambda e, t=t, par=par: e.tensor_copy(
                    sdbg[:, t:t + 1],
                    smv[:, C_RS + par:C_RS + par + 1]),
                    reads=sc(C_RS + par), writes=[("sdbg", t)])
            rs = C_RS + par
            rmx, rmn, da, db_, dc = (C_AD + par, 40 + par, C_NM + par,
                                     42 + par, 44 + par)
            p.op("vector", lambda e, xb=xb, rmx=rmx: e.reduce_max(
                smv[:, rmx:rmx + 1], xb[:], AX.X),
                reads=K1(xk[par]), writes=sc(rmx))
            p.op("vector", lambda e, xb=xb, rmn=rmn: e.tensor_reduce(
                smv[:, rmn:rmn + 1], xb[:], AX.X, ALU.min),
                reads=K1(xk[par]), writes=sc(rmn))
            p.op("vector", lambda e, rmx=rmx, da=da, par=par: e.tensor_sub(
                smv[:, da:da + 1], smv[:, rmx:rmx + 1],
                smv[:, C_M + par:C_M + par + 1]),
                reads=sc(rmx) + sc(C_M + par), writes=sc(da))
            p.op("vector", lambda e, rmn=rmn, db_=db_, par=par: e.tensor_sub(
                smv[:, db_:db_ + 1], smv[:, C_M + par:C_M + par + 1],
                smv[:, rmn:rmn + 1]),
                reads=sc(rmn) + sc(C_M + par), writes=sc(db_))
            p.op("vector", lambda e, da=da, db_=db_, dc=dc: e.tensor_tensor(
                smv[:, dc:dc + 1], smv[:, da:da + 1], smv[:, db_:db_ + 1],
                ALU.max),
                reads=sc(da) + sc(db_), writes=sc(dc))
            if DEBUG_L1:
                p.op("vector", lambda e, t=t, dc=dc: e.tensor_copy(
                    sdbg[:, NT_ALL + t:NT_ALL + t + 1],
                    smv[:, dc:dc + 1]),
                    reads=sc(dc), writes=[("sdbg", NT_ALL + t)])
            go, gi = (t + 1) % 2, t % 2
            p.op("vector", lambda e, dc=dc, rs=rs, go=go, gi=gi:
                 e.scalar_tensor_tensor(
                     gacc[:, go:go + 1], smv[:, dc:dc + 1], smv[:, rs:rs + 1],
                     gacc[:, gi:gi + 1], ALU.mult, ALU.max),
                 reads=sc(dc) + sc(rs) + [("gacc", gi)], writes=[("gacc", go)])

        def gmax_chain():
            p.op("vector", lambda e: e.tensor_copy(colz[:, 0:1], gacc[:, 0:1]),
                 reads=[("gacc", 0)] + K1("colz"), writes=K1("colz"))
            p.op("tensor", lambda e: e.transpose(pT1[:, 0:128], colz[:], idt[:]),
                 reads=K1("colz") + K1("idt"), writes=psk(pT1))
            p.op("vector", lambda e: e.reduce_max(gm1[:], pT1[0:1, 0:128], AX.X),
                 reads=psk(pT1), writes=K1("gm1"))
            p.op("scalar", lambda e: e.activation(
                smv[0:1, 31:32], clipb[:], AF.Sigmoid),
                reads=K1("clipb"), writes=sc(31))
            if DEBUG_L1:
                dsx = E(nc.sbuf_tensor("dsx", [1, 8], F32))
                p.op("vector", lambda e: e.memset(dsx[:], 0.0),
                     writes=[("dsx", j) for j in range(8)])
                p.op("vector", lambda e: e.tensor_copy(dsx[0:1, 2:3], gm1[:]),
                     reads=K1("gm1"), writes=[("dsx", 2)])
                p.op("vector", lambda e: e.tensor_copy(dsx[0:1, 3:4], smv[0:1, 31:32]),
                     reads=sc(31), writes=[("dsx", 3)])
            p.op("vector", lambda e: e.tensor_tensor(
                e0b[0:1, 0:1], smv[0:1, 31:32], gm1[:], ALU.mult),
                reads=sc(31) + K1("gm1") + K1("e0b"), writes=K1("e0b"))
            if DEBUG_L1:
                p.op("vector", lambda e: e.tensor_copy(dsx[0:1, 0:1], e0b[0:1, 0:1]),
                     reads=K1("e0b"), writes=[("dsx", 0)])
            if DEBUG_L1:
                p.op("vector", lambda e: e.tensor_copy(dsx[0:1, 1:2], e0b[0:1, 0:1]),
                     reads=K1("e0b"), writes=[("dsx", 1)])
                dsx_o = ap("d_sx", [1, 8], F32, kind="ExternalOutput")
                p.op("gpsimd", lambda e: e.dma_start(out=dsx_o, in_=dsx[:]),
                     reads=[("dsx", j) for j in range(4)], dma=True)
            p.op("tensor", lambda e: e.matmul(pT0[:, 0:1], onesb[:], e0b[:],
                                              start=True, stop=True),
                 reads=K1("onesb") + K1("e0b"), writes=psk(pT0))
            p.op("vector", lambda e: e.tensor_copy(sxb[:], pT0[:, 0:1]),
                 reads=psk(pT0), writes=K1("sxb"))
            p.op("vector", lambda e: e.reciprocal(rsxb[:], sxb[:]),
                 reads=K1("sxb"), writes=K1("rsxb"))
            p.op("vector", lambda e: e.tensor_scalar(
                rsxb[:], rsxb[:], QMAX, None, ALU.mult),
                reads=K1("rsxb"), writes=K1("rsxb"))
            p.op("vector", lambda e: e.reciprocal(sxb[:], rsxb[:]),
                 reads=K1("rsxb"), writes=K1("sxb"))


        for ot in range(NS):
            par = ot % 2
            xb, ab = xbs[par], abs_[par]
            p.op("sync", lambda e, xb=xb, ot=ot: e.dma_start(
                out=xb[:], in_=Xown[ot * 128:(ot + 1) * 128, :]),
                writes=K1(xk[par]), dma=True)
            tile_stats(xb, ab, par)
            p.op("vector", lambda e, xb=xb, ab=ab, par=par: e.tensor_scalar(
                ab[:], xb[:], smv[:, C_M + par:C_M + par + 1],
                smv[:, C_RS + par:C_RS + par + 1],
                ALU.subtract, ALU.mult),
                reads=K1(xk[par]) + sc(C_M + par) + sc(C_RS + par),
                writes=K1(akey[par]))
            for c in range(8):
                tp = (pT0, pT1, pT2, pT3)[(ot * 8 + c) % 4]
                col = c * TOK + ot * 128
                p.op("tensor", lambda e, ab=ab, c=c, tp=tp: e.transpose(
                    tp[:, 0:128], ab[:, c * 128:(c + 1) * 128], idt[:]),
                    reads=K1(akey[par]) + K1("idt"), writes=psk(tp))
                p.op("scalar", lambda e, tp=tp, col=col: e.activation(
                    xnF[:, col:col + 128], tp[:, 0:128], AF.Copy, scale=1.0),
                    reads=psk(tp), writes=[("xnF", col // 128)], cost=250)

        def load_w(W):
            for dd in range(8):
                p.op("sync", lambda e, dd=dd, W=W: e.dma_start(
                    out=A2[:, dd * D:(dd + 1) * D],
                    in_=W[dd * 128:(dd + 1) * 128, :]),
                    writes=K("A2", dd * D, (dd + 1) * D), dma=True)

        def proj(biasb, bkkey):
            for tb in range(4):
                for c2 in range(8):
                    k = tb * 8 + c2
                    ps = (pA, pB)[k % 2]
                    rd = []
                    for dd in range(8):
                        rd += K("A2", dd * D + c2 * 128, dd * D + (c2 + 1) * 128)
                        rd += K("xnF", dd * TOK + tb * 512,
                                dd * TOK + (tb + 1) * 512)

                    def mmf(e, tb=tb, c2=c2, ps=ps):
                        r = None
                        for dd in range(8):
                            r = e.matmul(
                                ps[:],
                                A2[:, dd * D + c2 * 128: dd * D + (c2 + 1) * 128],
                                xnF[:, dd * TOK + tb * 512:
                                    dd * TOK + (tb + 1) * 512],
                                start=(dd == 0), stop=(dd == 7))
                        return r
                    p.op("tensor", mmf, reads=rd, writes=psk(ps), cost=4500)
                    col = c2 * TOK + tb * 512
                    p.op("scalar", lambda e, ps=ps, col=col, c2=c2:
                         e.activation(A1[:, col:col + 512], ps[:],
                                      AF.Identity, bias=biasb[:, c2:c2 + 1],
                                      scale=1.0),
                         reads=psk(ps) + K1(bkkey),
                         writes=K("A1", col, col + 512), cost=600)

        load_w(Wq)
        proj(bqb, "bqb")

        for t in range(NT_SEQ):
            stat_tile(t)
        load_w(Wk)
        for kb in range(NS):
            for q in range(6):
                stat_tile(NT_SEQ + kb * 6 + q)
            for sh in range(2):
                par = sh
                xb, ab = xbs[par], abs_[par]
                t = kb * 2 + sh
                p.op("sync", lambda e, xb=xb, t=t: e.dma_start(
                    out=xb[:], in_=Xstat[t * 128:(t + 1) * 128, :]),
                    writes=K1(xk[par]), dma=True)
                p.op("vector", lambda e, xb=xb, ab=ab, t=t: e.tensor_scalar(
                    ab[:], xb[:], statsM[:, t:t + 1], statsR[:, t:t + 1],
                    ALU.subtract, ALU.mult),
                    reads=K1(xk[par]) + [("statsM", t), ("statsR", t)],
                    writes=K1(akey[par]))
                for c in range(8):
                    tp = (pT0, pT1, pT2, pT3)[(sh * 8 + c) % 4]
                    col = c * KB + sh * 128
                    p.op("tensor", lambda e, ab=ab, c=c, tp=tp: e.transpose(
                        tp[:, 0:128], ab[:, c * 128:(c + 1) * 128], idt[:]),
                        reads=K1(akey[par]) + K1("idt"), writes=psk(tp))
                    p.op("scalar", lambda e, tp=tp, col=col: e.activation(
                        A3[:, col:col + 128], tp[:, 0:128], AF.Copy,
                        scale=1.0),
                        reads=psk(tp), writes=[("A3", col // 128)])
            for c2 in range(8):
                ps = (pA, pB)[c2 % 2]
                rd = K("A3", 0, 2048)
                for dd in range(8):
                    rd += K("A2", dd * D + c2 * 128, dd * D + (c2 + 1) * 128)

                def kmf(e, c2=c2, ps=ps):
                    r = None
                    for dd in range(8):
                        r = e.matmul(
                            ps[:, 0:KB],
                            A2[:, dd * D + c2 * 128: dd * D + (c2 + 1) * 128],
                            A3[:, dd * KB:(dd + 1) * KB],
                            start=(dd == 0), stop=(dd == 7))
                    return r
                p.op("tensor", kmf, reads=rd, writes=psk(ps), cost=2300)
                col = 2048 + c2 * KB
                p.op("scalar", lambda e, ps=ps, col=col, c2=c2: e.activation(
                    A3[:, col:col + KB], ps[:, 0:KB], AF.Identity,
                    bias=bkb[:, c2:c2 + 1], scale=1.0),
                    reads=psk(ps) + K1("bkb"), writes=K("A3", col, col + KB),
                    cost=350)
            for si, j in enumerate(range(kb, NS)):
                ps = (pC, pD)[si % 2]
                par = si % 2
                nm_c, mn_c, dm_c, sb_c = 18 + par, 20 + par, 22 + par, 24 + par
                rd = K("A3", 2048, 4096)
                for dd in range(8):
                    rd += K("A1", dd * TOK + j * 128, dd * TOK + (j + 1) * 128)

                def smf(e, j=j, ps=ps):
                    r = None
                    for dd in range(8):
                        r = e.matmul(
                            ps[:, 0:KB],
                            A1[:, dd * TOK + j * 128: dd * TOK + (j + 1) * 128],
                            A3[:, 2048 + dd * KB: 2048 + (dd + 1) * KB],
                            start=(dd == 0), stop=(dd == 7))
                    return r
                p.op("tensor", smf, reads=rd, writes=psk(ps), cost=2300)
                masked = (kb == j)
                if masked:
                    mh = 0 if j < 8 else 1
                    p.op("vector", lambda e, ps=ps, mh=mh: e.tensor_tensor(
                        sSb[:], ps[:, 0:KB], maskb[:, mh * KB:(mh + 1) * KB],
                        ALU.mult),
                        reads=psk(ps) + K1("maskb"), writes=K1("sSb"))
                    src = sSb[:]
                    srck = K1("sSb")
                else:
                    src = ps[:, 0:KB]
                    srck = psk(ps)
                p.op("vector", lambda e, src=src, nm_c=nm_c: e.reduce_max(
                    smv[:, nm_c:nm_c + 1], src, AX.X, negate=True),
                    reads=srck, writes=sc(nm_c))
                if kb == 0:
                    p.op("vector", lambda e, j=j, nm_c=nm_c: e.tensor_scalar(
                        negM[:, j:j + 1], smv[:, nm_c:nm_c + 1], 1.0 / 32.0,
                        None, ALU.mult),
                        reads=sc(nm_c), writes=[("negM", j)])
                    p.op("scalar", lambda e, src=src, j=j: e.activation(
                        expb[:], src, AF.Exp, bias=negM[:, j:j + 1],
                        scale=1.0 / 32.0, accum_out=Db[:, j:j + 1]),
                        reads=srck + [("negM", j)] + K1("expb"),
                        writes=K1("expb") + [("Db", j)])
                else:
                    p.op("vector", lambda e, nm_c=nm_c: e.tensor_scalar(
                        smv[:, nm_c:nm_c + 1], smv[:, nm_c:nm_c + 1],
                        1.0 / 32.0, None, ALU.mult),
                        reads=sc(nm_c), writes=sc(nm_c))
                    p.op("vector", lambda e, j=j, nm_c=nm_c, mn_c=mn_c:
                         e.tensor_tensor(smv[:, mn_c:mn_c + 1],
                                         negM[:, j:j + 1],
                                         smv[:, nm_c:nm_c + 1], ALU.min),
                         reads=sc(nm_c) + [("negM", j)], writes=sc(mn_c))
                    p.op("vector", lambda e, j=j, mn_c=mn_c, dm_c=dm_c:
                         e.tensor_sub(smv[:, dm_c:dm_c + 1],
                                      smv[:, mn_c:mn_c + 1], negM[:, j:j + 1]),
                         reads=sc(mn_c) + [("negM", j)], writes=sc(dm_c))
                    p.op("vector", lambda e, j=j, mn_c=mn_c: e.tensor_copy(
                        negM[:, j:j + 1], smv[:, mn_c:mn_c + 1]),
                        reads=sc(mn_c), writes=[("negM", j)])
                    p.op("scalar", lambda e, dm_c=dm_c: e.activation(
                        smv[:, dm_c:dm_c + 1], smv[:, dm_c:dm_c + 1], AF.Exp),
                        reads=sc(dm_c), writes=sc(dm_c))
                    p.op("scalar", lambda e, src=src, j=j, sb_c=sb_c:
                         e.activation(expb[:], src, AF.Exp,
                                      bias=negM[:, j:j + 1], scale=1.0 / 32.0,
                                      accum_out=smv[:, sb_c:sb_c + 1]),
                         reads=srck + [("negM", j)] + K1("expb"),
                         writes=K1("expb") + sc(sb_c))
                    p.op("vector", lambda e, j=j, dm_c=dm_c, sb_c=sb_c:
                         e.scalar_tensor_tensor(
                             Db[:, j:j + 1], Db[:, j:j + 1],
                             smv[:, dm_c:dm_c + 1], smv[:, sb_c:sb_c + 1],
                             ALU.mult, ALU.add),
                         reads=[("Db", j)] + sc(dm_c) + sc(sb_c),
                         writes=[("Db", j)])

        for j in range(NS):
            p.op("scalar", lambda e, j=j: e.activation(
                e0s[:, j:j + 1], negM[:, j:j + 1], AF.Exp),
                reads=[("negM", j)], writes=[("e0s", j)])
            cnt = float(S - KB * (j + 1))
            p.op("vector", lambda e, j=j, cnt=cnt: e.scalar_tensor_tensor(
                Db[:, j:j + 1], e0s[:, j:j + 1], cnt, Db[:, j:j + 1],
                ALU.mult, ALU.add),
                reads=[("e0s", j), ("Db", j)], writes=[("Db", j)])
        p.op("vector", lambda e: e.reciprocal(sSb[:, 0:NS], Db[:, 0:NS]),
             reads=[("Db", j) for j in range(NS)] + K1("sSb"), writes=K1("sSb"))
        p.op("vector", lambda e: e.tensor_tensor(
            rb[:], Db[:], sSb[:, 0:NS], ALU.mult),
            reads=K1("sSb") + [("Db", j) for j in range(NS)], writes=K1("rb"))

        gmax_chain()

        load_w(Wv)
        proj(bvb, "bvb")
        for ot in range(NS):
            par = ot % 2
            xb = xbs[par]
            gofs = par * 1024
            gk = K("A3", gofs, gofs + 1024)
            p.op("sync", lambda e, xb=xb, ot=ot: e.dma_start(
                out=xb[:], in_=Xown[ot * 128:(ot + 1) * 128, :]),
                writes=K1(xk[par]), dma=True)
            for c in range(8):
                tp = (pT0, pT1, pT2, pT3)[(ot * 8 + c) % 4]
                col = c * TOK + ot * 128
                p.op("tensor", lambda e, col=col, tp=tp: e.transpose(
                    tp[:, 0:128], A1[:, col:col + 128], idt[:]),
                    reads=K("A1", col, col + 128) + K1("idt"), writes=psk(tp))
                p.op("scalar", lambda e, tp=tp, c=c, gofs=gofs: e.activation(
                    A3[:, gofs + c * 128: gofs + (c + 1) * 128], tp[:, 0:128],
                    AF.Copy, scale=1.0),
                    reads=psk(tp),
                    writes=[("A3", (gofs + c * 128) // 128)], cost=250)
            p.op("vector", lambda e, gofs=gofs, ot=ot, xb=xb:
                 e.scalar_tensor_tensor(
                     A3[:, gofs:gofs + 1024], A3[:, gofs:gofs + 1024],
                     rb[:, ot:ot + 1], xb[:], ALU.mult, ALU.add),
                 reads=gk + K1("rb") + K1(xk[par]), writes=gk)
            iofs = 2048 + par * 1024
            ik = K("A3", iofs, iofs + 1024)
            p.op("vector", lambda e, gofs=gofs: e.tensor_scalar(
                A5[:], A3[:, gofs:gofs + 1024], G0, None, ALU.mult),
                reads=gk, writes=K1("A5"))
            p.op("vector", lambda e: e.tensor_copy(A4[:], A5[:]),
                 reads=K1("A5"), writes=K1("A4"))
            p.op("vector", lambda e: e.tensor_copy(ab0[:], A4[:]),
                 reads=K1("A4"), writes=K1("ak0"))
            p.op("vector", lambda e, iofs=iofs: e.tensor_scalar(
                A3[:, iofs:iofs + 1024], ab0[:], 1.0, None, ALU.add),
                reads=K1("ak0"), writes=ik)
            p.op("vector", lambda e: e.tensor_sub(ab1[:], A5[:], ab0[:]),
                 reads=K1("A5") + K1("ak0"), writes=K1("ak1"))
            p.op("vector", lambda e: e.tensor_scalar(
                ab0[:], ab1[:], G1, None, ALU.mult),
                reads=K1("ak1"), writes=K1("ak0"))
            p.op("gpsimd", lambda e, iofs=iofs, ot=ot: e.dma_start(
                out=acc_o[ot * 128:(ot + 1) * 128, :],
                in_=A3[:, iofs:iofs + 1024]),
                reads=ik, dma=True)
            p.op("gpsimd", lambda e, ot=ot: e.dma_start(
                out=acc1_o[ot * 128:(ot + 1) * 128, :], in_=ab0[:]),
                reads=K1("ak0"), dma=True)

        for c in range(8):
            for tb in range(4):
                lo = c * TOK + tb * 512
                xnk = K("xnF", lo, lo + 512)
                p.op("vector", lambda e, lo=lo: e.tensor_scalar(
                    A3[:, 0:512], xnF[:, lo:lo + 512], rsxb[:], None, ALU.mult),
                    reads=xnk + K1("rsxb"), writes=K("A3", 0, 512))
                p.op("vector", lambda e: e.tensor_copy(A4[:, 0:512], A3[:, 0:512]),
                     reads=K("A3", 0, 512), writes=K1("A4"))
                p.op("vector", lambda e: e.tensor_copy(A5[:, 0:512], A4[:, 0:512]),
                     reads=K1("A4"), writes=K1("A5"))
                p.op("vector", lambda e: e.tensor_sub(
                    A3[:, 512:1024], A3[:, 0:512], A5[:, 0:512]),
                    reads=K("A3", 0, 512) + K1("A5"), writes=K("A3", 512, 1024))
                p.op("vector", lambda e: e.scalar_tensor_tensor(
                    A5[:, 0:512], A3[:, 512:1024], 0.5, A5[:, 0:512],
                    ALU.is_gt, ALU.add),
                    reads=K("A3", 512, 1024) + K1("A5"), writes=K1("A5"))
                p.op("vector", lambda e: e.scalar_tensor_tensor(
                    A5[:, 0:512], A3[:, 512:1024], -0.5, A5[:, 0:512],
                    ALU.is_ge, ALU.add),
                    reads=K("A3", 512, 1024) + K1("A5"), writes=K1("A5"))
                p.op("vector", lambda e: e.tensor_scalar(
                    A5[:, 0:512], A5[:, 0:512], 1.0, QMAX, ALU.subtract, ALU.min),
                    reads=K1("A5"), writes=K1("A5"))
                p.op("vector", lambda e, lo=lo: e.tensor_scalar(
                    xnF[:, lo:lo + 512], A5[:, 0:512], -QMAX - 1.0, sxb[:],
                    ALU.max, ALU.mult),
                    reads=K1("A5") + K1("sxb"), writes=xnk)

        WGT, WUT = 4096, 5120
        for ic in range(NIC):
            par = ic % 2
            ng = par * 1024
            nu = 2048 + par * 1024
            ngk = K("A2", ng, ng + 1024)
            nuk = K("A2", nu, nu + 1024)
            p.op("sync", lambda e, ic=ic, ng=ng: e.dma_start(
                out=A2[:, ng:ng + 1024], in_=Wg[ic * 128:(ic + 1) * 128, :]),
                writes=ngk, dma=True)
            p.op("sync", lambda e, ic=ic, nu=nu: e.dma_start(
                out=A2[:, nu:nu + 1024], in_=Wu[ic * 128:(ic + 1) * 128, :]),
                writes=nuk, dma=True)
            p.defc = 450
            for half, (no, nk, wofs) in enumerate(
                    ((ng, ngk, WGT), (nu, nuk, WUT))):
                wc = 26 + par * 2 + half
                rc = 32 + par * 2 + half
                nmc = 36 + par * 2 + half
                p.op("vector", lambda e, no=no, wc=wc: e.reduce_max(
                    smv[:, wc:wc + 1], A2[:, no:no + 1024], AX.X,
                    apply_absolute_value=True),
                    reads=nk, writes=sc(wc))
                p.op("vector", lambda e, wc=wc, rc=rc: e.reciprocal(
                    smv[:, rc:rc + 1], smv[:, wc:wc + 1]),
                    reads=sc(wc), writes=sc(rc))
                p.op("vector", lambda e, wc=wc, nmc=nmc: e.tensor_scalar(
                    smv[:, nmc:nmc + 1], smv[:, wc:wc + 1], -1.0, None,
                    ALU.mult), reads=sc(wc), writes=sc(nmc))
                p.op("vector", lambda e, rc=rc: e.tensor_scalar(
                    smv[:, 30:31], smv[:, rc:rc + 1], QMAX, None, ALU.mult),
                    reads=sc(rc), writes=sc(30))
                p.op("vector", lambda e, wc=wc: e.reciprocal(
                    smv[:, wc:wc + 1], smv[:, 30:31]),
                    reads=sc(30), writes=sc(wc))
                kta = K1("xk0")
                ktb = K1("xk1")
                ktc = K("A1", 0, 1024)
                p.op("scalar", lambda e, no=no: e.activation(
                    xb0[:], A2[:, no:no + 1024], AF.Copy, scale=QMAX),
                    reads=nk, writes=kta, cost=450)
                p.op("vector", lambda e, rc=rc: e.tensor_scalar(
                    xb1[:], xb0[:], smv[:, rc:rc + 1], None, ALU.mult),
                    reads=kta + sc(rc), writes=ktb)
                p.op("vector", lambda e, nmc=nmc: e.scalar_tensor_tensor(
                    A1[:, 0:1024], xb1[:], smv[:, nmc:nmc + 1], xb0[:],
                    ALU.mult, ALU.add),
                    reads=ktb + kta + sc(nmc), writes=ktc)
                p.op("vector", lambda e, rc=rc: e.scalar_tensor_tensor(
                    xb1[:], A1[:, 0:1024], smv[:, rc:rc + 1], xb1[:],
                    ALU.mult, ALU.add),
                    reads=ktc + ktb + sc(rc), writes=ktb, cost=700)
                p.op("vector", lambda e, nmc=nmc: e.scalar_tensor_tensor(
                    A1[:, 0:1024], xb1[:], smv[:, nmc:nmc + 1], xb0[:],
                    ALU.mult, ALU.add),
                    reads=ktb + kta + sc(nmc), writes=ktc)
                p.op("vector", lambda e, rc=rc: e.scalar_tensor_tensor(
                    xb1[:], A1[:, 0:1024], smv[:, rc:rc + 1], xb1[:],
                    ALU.mult, ALU.add),
                    reads=ktc + ktb + sc(rc), writes=ktb, cost=700)
                p.op("vector", lambda e: e.tensor_copy(A4[:], xb1[:]),
                     reads=ktb, writes=K1("A4"))
                p.op("vector", lambda e: e.tensor_copy(xb0[:], A4[:]),
                     reads=K1("A4"), writes=kta)
                p.op("vector", lambda e: e.tensor_sub(
                    A1[:, 0:1024], xb1[:], xb0[:]),
                    reads=ktb + kta, writes=ktc)
                p.op("vector", lambda e: e.scalar_tensor_tensor(
                    xb0[:], A1[:, 0:1024], 0.5, xb0[:], ALU.is_ge, ALU.add),
                    reads=ktc + kta, writes=kta, cost=700)
                p.op("vector", lambda e: e.scalar_tensor_tensor(
                    xb0[:], A1[:, 0:1024], -0.5, xb0[:], ALU.is_ge, ALU.add),
                    reads=ktc + kta, writes=kta, cost=700)
                p.op("vector", lambda e: e.scalar_tensor_tensor(
                    A1[:, 0:1024], xb0[:], -1.0, xb1[:],
                    ALU.add, ALU.subtract),
                    reads=kta + ktb, writes=ktc)
                p.op("vector", lambda e: e.tensor_scalar(
                    A1[:, 0:1024], A1[:, 0:1024], 0.5, None, ALU.is_ge),
                    reads=ktc, writes=ktc, cost=700)
                p.op("scalar", lambda e: e.activation(
                    xb1[:], xb0[:], AF.Copy, bias=-0.5, scale=0.5),
                    reads=kta, writes=ktb, cost=700)
                for hh in range(4):
                    ha = hh * 256
                    hb = ((hh + 1) % 4) * 256
                    p.op("vector", lambda e, ha=ha: e.tensor_copy(
                        wtp[:], xb1[:, ha:ha + 256]),
                        reads=ktb, writes=K1("wtp"), cost=200)
                    p.op("vector", lambda e, hb=hb: e.tensor_copy(
                        xb1[:, hb:hb + 256], wtp[:]),
                        reads=K1("wtp"), writes=ktb, cost=200)
                    p.op("vector", lambda e, ha=ha, hb=hb: e.tensor_sub(
                        xb1[:, ha:ha + 256], xb1[:, ha:ha + 256],
                        xb1[:, hb:hb + 256]),
                        reads=ktb, writes=ktb, cost=300)
                    p.op("vector", lambda e, ha=ha: e.tensor_tensor(
                        xb1[:, ha:ha + 256], xb1[:, ha:ha + 256],
                        A1[:, ha:ha + 256], ALU.mult),
                        reads=ktb + ktc, writes=ktb, cost=300)
                    p.op("vector", lambda e, ha=ha: e.tensor_tensor(
                        xb1[:, ha:ha + 256], xb1[:, ha:ha + 256],
                        xb1[:, ha:ha + 256], ALU.mult),
                        reads=ktb, writes=ktb, cost=300)
                    p.op("vector", lambda e, ha=ha: e.scalar_tensor_tensor(
                        xb0[:, ha:ha + 256], xb1[:, ha:ha + 256], -4.0,
                        xb0[:, ha:ha + 256], ALU.mult, ALU.add),
                        reads=ktb + kta, writes=kta, cost=300)
                p.op("vector", lambda e, no=no, wc=wc: e.tensor_scalar(
                    A2[:, no:no + 1024], xb0[:], 1.0,
                    smv[:, wc:wc + 1], ALU.subtract, ALU.mult),
                    reads=kta + sc(wc), writes=nk)
                p.defc = None
                for c in range(8):
                    tp = (pT0, pT1, pT2, pT3)[c % 4]
                    p.op("tensor", lambda e, no=no, c=c, tp=tp: e.transpose(
                        tp[:, 0:128], A2[:, no + c * 128: no + (c + 1) * 128],
                        idt[:]),
                        reads=K("A2", no + c * 128, no + (c + 1) * 128)
                        + K1("idt"), writes=psk(tp))
                    p.op("scalar", lambda e, tp=tp, wofs=wofs, c=c:
                         e.activation(A2[:, wofs + c * 128:
                                         wofs + (c + 1) * 128], tp[:, 0:128],
                                      AF.Copy, scale=1.0),
                         reads=psk(tp),
                         writes=K("A2", wofs + c * 128, wofs + (c + 1) * 128),
                         cost=250)
            wgtk = K("A2", WGT, WGT + 1024)
            wutk = K("A2", WUT, WUT + 1024)
            for tb in range(4):
                it = ic * 4 + tb
                pg = (pA, pB)[it % 2]
                pu = (pC, pD)[it % 2]
                xqk = []
                for dd in range(8):
                    xqk += K("xnF", dd * TOK + tb * 512, dd * TOK + (tb + 1) * 512)

                def gmf(e, tb=tb, ps=pg):
                    r = None
                    for dd in range(8):
                        r = e.matmul(
                            ps[:], A2[:, WGT + dd * 128: WGT + (dd + 1) * 128],
                            xnF[:, dd * TOK + tb * 512: dd * TOK + (tb + 1) * 512],
                            start=(dd == 0), stop=(dd == 7))
                    return r

                def umf(e, tb=tb, ps=pu):
                    r = None
                    for dd in range(8):
                        r = e.matmul(
                            ps[:], A2[:, WUT + dd * 128: WUT + (dd + 1) * 128],
                            xnF[:, dd * TOK + tb * 512: dd * TOK + (tb + 1) * 512],
                            start=(dd == 0), stop=(dd == 7))
                    return r
                p.op("tensor", gmf, reads=wgtk + xqk, writes=psk(pg), cost=4500)
                p.op("tensor", umf, reads=wutk + xqk, writes=psk(pu), cost=4500)
                p.op("scalar", lambda e, pg=pg: e.activation(
                    A3[:, 0:512], pg[:], AF.Silu),
                    reads=psk(pg), writes=K("A3", 0, 512))
                p.op("vector", lambda e, pu=pu: e.tensor_tensor(
                    A3[:, 512:1024], A3[:, 0:512], pu[:], ALU.mult),
                    reads=K("A3", 0, 512) + psk(pu), writes=K("A3", 512, 1024))
                p.op("vector", lambda e: e.reduce_max(
                    smv[:, 30:31], A3[:, 512:1024], AX.X,
                    apply_absolute_value=True),
                    reads=K("A3", 512, 1024), writes=sc(30))
                ho, hi_ = (it + 1) % 2, it % 2
                p.op("vector", lambda e, ho=ho, hi_=hi_: e.tensor_tensor(
                    hmacc[:, ho:ho + 1], hmacc[:, hi_:hi_ + 1],
                    smv[:, 30:31], ALU.max),
                    reads=sc(30) + [("hmacc", hi_)], writes=[("hmacc", ho)])
                p.op("scalar", lambda e: e.activation(
                    A3[:, 1024:1536], A3[:, 512:1024], AF.Copy, scale=G0),
                    reads=K("A3", 512, 1024), writes=K("A3", 1024, 1536))
                p.op("vector", lambda e: e.tensor_copy(
                    A4[:, 0:512], A3[:, 1024:1536]),
                    reads=K("A3", 1024, 1536), writes=K1("A4"))
                p.op("vector", lambda e: e.tensor_copy(
                    A5[:, 0:512], A4[:, 0:512]),
                    reads=K1("A4"), writes=K1("A5"))
                p.op("vector", lambda e: e.tensor_scalar(
                    A3[:, 2048:2560], A5[:, 0:512], 1.0, None, ALU.add),
                    reads=K1("A5"), writes=K("A3", 2048, 2560))
                p.op("vector", lambda e: e.tensor_sub(
                    A3[:, 1536:2048], A3[:, 1024:1536], A5[:, 0:512]),
                    reads=K("A3", 1024, 1536) + K1("A5"),
                    writes=K("A3", 1536, 2048))
                p.op("vector", lambda e: e.tensor_scalar(
                    A3[:, 2560:3072], A3[:, 1536:2048], G1, None, ALU.mult),
                    reads=K("A3", 1536, 2048), writes=K("A3", 2560, 3072))
                p.op("gpsimd", lambda e, ic=ic, tb=tb: e.dma_start(
                    out=hi0_o[ic * 128:(ic + 1) * 128, tb * 512:(tb + 1) * 512],
                    in_=A3[:, 2048:2560]),
                    reads=K("A3", 2048, 2560), dma=True)
                p.op("gpsimd", lambda e, ic=ic, tb=tb: e.dma_start(
                    out=hi1_o[ic * 128:(ic + 1) * 128, tb * 512:(tb + 1) * 512],
                    in_=A3[:, 2560:3072]),
                    reads=K("A3", 2560, 3072), dma=True)

        p.op("vector", lambda e: e.tensor_copy(colz[:, 0:1], hmacc[:, 0:1]),
             reads=[("hmacc", 0)] + K1("colz"), writes=K1("colz"))
        p.op("tensor", lambda e: e.transpose(pT1[:, 0:128], colz[:], idt[:]),
             reads=K1("colz") + K1("idt"), writes=psk(pT1))
        p.op("vector", lambda e: e.reduce_max(hm1[:], pT1[0:1, 0:128], AX.X),
             reads=psk(pT1), writes=K1("hm1"))
        p.op("vector", lambda e: e.tensor_scalar(
            smv[0:1, 18:19], hm1[:], G0, None, ALU.mult),
            reads=K1("hm1"), writes=sc(18))
        p.op("vector", lambda e: e.tensor_copy(A4[0:1, 0:1], smv[0:1, 18:19]),
             reads=sc(18), writes=K1("A4"))
        p.op("vector", lambda e: e.tensor_copy(smv[0:1, 19:20], A4[0:1, 0:1]),
             reads=K1("A4"), writes=sc(19))
        p.op("vector", lambda e: e.tensor_sub(
            smv[0:1, 20:21], smv[0:1, 18:19], smv[0:1, 19:20]),
            reads=sc(18) + sc(19), writes=sc(20))
        p.op("vector", lambda e: e.scalar_tensor_tensor(
            smv[0:1, 21:22], smv[0:1, 20:21], 0.5, smv[0:1, 19:20],
            ALU.is_gt, ALU.add),
            reads=sc(20) + sc(19), writes=sc(21))
        p.op("vector", lambda e: e.scalar_tensor_tensor(
            smv[0:1, 21:22], smv[0:1, 20:21], -0.5, smv[0:1, 21:22],
            ALU.is_ge, ALU.add),
            reads=sc(20) + sc(21), writes=sc(21))
        p.op("vector", lambda e: e.tensor_copy(hmout[0:1, 0:1], smv[0:1, 21:22]),
             reads=sc(21) + K1("hmout"), writes=K1("hmout"))
        p.op("vector", lambda e: e.scalar_tensor_tensor(
            smv[0:1, 22:23], smv[0:1, 21:22], -1.0, smv[0:1, 18:19],
            ALU.add, ALU.subtract),
            reads=sc(21) + sc(18), writes=sc(22))
        p.op("vector", lambda e: e.tensor_scalar(
            hmout[0:1, 1:2], smv[0:1, 22:23], -G1, None, ALU.mult),
            reads=sc(22) + K1("hmout"), writes=K1("hmout"))
        p.op("gpsimd", lambda e: e.dma_start(out=hm_o, in_=hmout[:]),
             reads=K1("hmout"), dma=True)

        if DEBUG_L1:
            dbg = {
                "d_gacc": (gacc, [128, 2], [("gacc", 0), ("gacc", 1)]),
                "d_sxb": (sxb, [128, 1], K1("sxb")),
                "d_negM": (negM, [128, NS], [("negM", j) for j in range(NS)]),
                "d_Db": (Db, [128, NS], [("Db", j) for j in range(NS)]),
                "d_rb": (rb, [128, NS], K1("rb")),
                "d_hmacc": (hmacc, [128, 2], [("hmacc", 0), ("hmacc", 1)]),
                "d_e0b": (e0b, [128, 1], K1("e0b")),
                "d_gm1": (gm1, [1, 1], K1("gm1")),
                "d_smv": (smv, [128, 32], [("smv", c) for c in range(32)]),
                "d_stM": (statsM, [128, 32],
                          [("statsM", t) for t in range(NT_SEQ)]),
                "d_stR": (statsR, [128, 32],
                          [("statsR", t) for t in range(NT_SEQ)]),
                "d_sdbg": (sdbg, [128, 2 * NT_ALL],
                           [("sdbg", j) for j in range(2 * NT_ALL)]),
            }
            for nm, (t, shp, keys) in dbg.items():
                o = ap(nm, shp, F32, kind="ExternalOutput")
                p.op("gpsimd", lambda e, o=o, t=t: e.dma_start(out=o, in_=t[:]),
                     reads=keys, dma=True)
            for nm, src, w, keys in (
                    ("d_xn0", xnF[:, 0:TOK], TOK, K("xnF", 0, TOK)),
                    ("d_v0", A1[:, 0:TOK], TOK, K("A1", 0, TOK)),
                    ("d_wgq", A2[:, 4096:5120], 1024, K("A2", 4096, 5120)),
                    ("d_kO", A3[:, 2048:4096], 2048, K("A3", 2048, 4096))):
                o = ap(nm, [128, w], F32, kind="ExternalOutput")
                p.op("gpsimd", lambda e, o=o, src=src: e.dma_start(out=o, in_=src),
                     reads=keys, dma=True)

        p.emit_into(block, sems, dsems)
    return nc



def build_l2():
    nc = bass.Bass("TRN2", target_bir_lowering=False)
    ap = lambda *a, **k: nc.dram_tensor(*a, **k).ap()

    hi0 = ap("hi0", [II, TOK], F32, kind="ExternalInput")
    hi1 = ap("hi1", [II, TOK], F32, kind="ExternalInput")
    acc = ap("acc", [TOK, D], F32, kind="ExternalInput")
    acc1 = ap("acc1", [TOK, D], F32, kind="ExternalInput")
    hm8 = ap("hm8", [8, 8], F32, kind="ExternalInput")
    Wd = ap("Wd", [D, II], F32, kind="ExternalInput")
    clipd = ap("clipd", [1, 1], F32, kind="ExternalInput")
    ident = ap("ident", [128, 128], F32, kind="ExternalInput")
    out_o = ap("out", [TOK, D], F32, kind="ExternalOutput")

    p = Prog()
    with ExitStack() as es:
        E = es.enter_context
        wdT = E(nc.sbuf_tensor("wdT", [128, 8 * II], F32))
        natd = E(nc.sbuf_tensor("natd", [128, II], F32))
        hqB = E(nc.sbuf_tensor("hqB", [128, NIC * 512], F32))
        ld0 = E(nc.sbuf_tensor("ld0", [128, 1024], F32))
        ld1 = E(nc.sbuf_tensor("ld1", [128, 1024], F32))
        dF = E(nc.sbuf_tensor("dF", [128, 8 * 512], F32))
        oT = E(nc.sbuf_tensor("oT", [128, D], F32))
        aT = E(nc.sbuf_tensor("aT", [128, D], F32))
        aT1 = E(nc.sbuf_tensor("aT1", [128, D], F32))
        aT2 = E(nc.sbuf_tensor("aT2", [128, D], F32))
        A4 = E(nc.sbuf_tensor("A4", [128, 1408], I32))
        A5 = E(nc.sbuf_tensor("A5", [128, 1408], F32))
        A6 = E(nc.sbuf_tensor("A6", [128, 1408], F32))
        d4 = E(nc.sbuf_tensor("d4", [128, 512], I32))
        wtp2 = E(nc.sbuf_tensor("wtp2", [128, 704], I32))
        smv = E(nc.sbuf_tensor("smv", [128, 16], F32))
        hmv = E(nc.sbuf_tensor("hmv", [8, 8], F32))
        hmg = E(nc.sbuf_tensor("hmg", [1, 1], F32))
        onesb = E(nc.sbuf_tensor("onesb", [128, 128], F32))
        e0b = E(nc.sbuf_tensor("e0b", [128, 1], F32))
        sdb = E(nc.sbuf_tensor("sdb", [128, 1], F32))
        rs0 = E(nc.sbuf_tensor("rs0", [128, 1], F32))
        s1b = E(nc.sbuf_tensor("s1b", [128, 1], F32))
        s2b = E(nc.sbuf_tensor("s2b", [128, 1], F32))
        clipb = E(nc.sbuf_tensor("clipb", [1, 1], F32))
        idt = E(nc.sbuf_tensor("idt", [128, 128], F32))
        colz = E(nc.sbuf_tensor("colz", [128, 128], F32))
        pA = E(nc.psum_tensor("pA", [128, 512], F32))
        pB = E(nc.psum_tensor("pB", [128, 512], F32))
        pT0 = E(nc.psum_tensor("pT0", [128, 512], F32))
        pT1 = E(nc.psum_tensor("pT1", [128, 512], F32))
        pT2 = E(nc.psum_tensor("pT2", [128, 512], F32))
        pT3 = E(nc.psum_tensor("pT3", [128, 512], F32))
        sems = {e: E(nc.semaphore("s_" + e)) for e in ENGINES}
        dsems = {e: [E(nc.semaphore(f"d_{e}{k}")) for k in range(Prog.DMA_LANES)]
                 for e in ("sync", "scalar", "gpsimd")}
        block = E(nc.Block())

        PS = {id(pA): "pA", id(pB): "pB", id(pT0): "pT0", id(pT1): "pT1",
              id(pT2): "pT2", id(pT3): "pT3"}
        psk = lambda t: [(PS[id(t)], 0)]
        sc = lambda c: [("smv", c)]

        p.op("sync", lambda e: e.dma_start(out=idt[:], in_=ident),
             writes=K1("idt"), dma=True)
        p.op("sync", lambda e: e.dma_start(out=clipb[:], in_=clipd),
             writes=K1("clipb"), dma=True)
        p.op("sync", lambda e: e.dma_start(out=hmv[:], in_=hm8),
             writes=K1("hmv"), dma=True)
        p.op("vector", lambda e: e.memset(onesb[:], 1.0), writes=K1("onesb"))
        p.op("vector", lambda e: e.memset(e0b[:], 0.0), writes=K1("e0b"))
        p.op("vector", lambda e: e.memset(colz[:], 0.0), writes=K1("colz"))
        p.op("vector", lambda e: e.memset(smv[:], 0.0),
             writes=[("smv", j) for j in range(32)])

        p.op("vector", lambda e: e.tensor_scalar(
            hmv[:, 2:3], hmv[:, 0:1], G0R, G0R, ALU.mult, ALU.subtract),
            reads=K1("hmv"), writes=K1("hmv"))
        p.op("vector", lambda e: e.scalar_tensor_tensor(
            hmv[:, 3:4], hmv[:, 1:2], G1R, hmv[:, 2:3], ALU.mult, ALU.add),
            reads=K1("hmv"), writes=K1("hmv"))
        p.op("vector", lambda e: e.tensor_copy(colz[0:8, 0:1], hmv[:, 3:4]),
             reads=K1("hmv") + K1("colz"), writes=K1("colz"))
        p.op("tensor", lambda e: e.transpose(pT1[:, 0:128], colz[:], idt[:]),
             reads=K1("colz") + K1("idt"), writes=psk(pT1))
        p.op("vector", lambda e: e.reduce_max(hmg[:], pT1[0:1, 0:128], AX.X),
             reads=psk(pT1), writes=K1("hmg"))
        p.op("scalar", lambda e: e.activation(
            smv[0:1, 0:1], clipb[:], AF.Sigmoid),
            reads=K1("clipb"), writes=sc(0))
        p.op("vector", lambda e: e.tensor_tensor(
            e0b[0:1, 0:1], smv[0:1, 0:1], hmg[:], ALU.mult),
            reads=sc(0) + K1("hmg") + K1("e0b"), writes=K1("e0b"))
        p.op("tensor", lambda e: e.matmul(pT0[:, 0:1], onesb[:], e0b[:],
                                          start=True, stop=True),
             reads=K1("onesb") + K1("e0b"), writes=psk(pT0))
        p.op("vector", lambda e: e.tensor_copy(sdb[:], pT0[:, 0:1]),
             reads=psk(pT0), writes=K1("sdb"))
        p.op("vector", lambda e: e.reciprocal(rs0[:], sdb[:]),
             reads=K1("sdb"), writes=K1("rs0"))
        p.op("vector", lambda e: e.tensor_scalar(
            rs0[:], rs0[:], QMAX, None, ALU.mult),
            reads=K1("rs0"), writes=K1("rs0"))
        p.op("vector", lambda e: e.reciprocal(sdb[:], rs0[:]),
             reads=K1("rs0"), writes=K1("sdb"))
        p.op("vector", lambda e: e.tensor_scalar(
            s1b[:], rs0[:], G0R, None, ALU.mult),
            reads=K1("rs0"), writes=K1("s1b"))
        p.op("vector", lambda e: e.tensor_scalar(
            s2b[:], rs0[:], G1R, None, ALU.mult),
            reads=K1("rs0"), writes=K1("s2b"))

        def decode_tile(tb, ic):
            par = ic % 2
            l0 = (ld0, ld1)[par]
            p.op("sync", lambda e, ic=ic, tb=tb, l0=l0: e.dma_start(
                out=l0[:, 0:512],
                in_=hi0[ic * 128:(ic + 1) * 128, tb * 512:(tb + 1) * 512]),
                writes=[("ld", par, 0)], dma=True)
            p.op("sync", lambda e, ic=ic, tb=tb, l0=l0: e.dma_start(
                out=l0[:, 512:1024],
                in_=hi1[ic * 128:(ic + 1) * 128, tb * 512:(tb + 1) * 512]),
                writes=[("ld", par, 1)], dma=True)
            kla = [("ld", par, 0)]
            klb = [("ld", par, 1)]
            p.op("vector", lambda e, l0=l0: e.tensor_scalar(
                l0[:, 0:512], l0[:, 0:512], s1b[:], s1b[:],
                ALU.mult, ALU.subtract),
                reads=kla + K1("s1b"), writes=kla, cost=250)
            p.op("vector", lambda e, l0=l0: e.scalar_tensor_tensor(
                l0[:, 0:512], l0[:, 512:1024], s2b[:], l0[:, 0:512],
                ALU.mult, ALU.add),
                reads=klb + kla + K1("s2b"), writes=kla, cost=250)
            p.op("vector", lambda e, l0=l0: e.tensor_copy(
                d4[:], l0[:, 0:512]),
                reads=kla, writes=K1("d4"), cost=250)
            p.op("vector", lambda e, l0=l0: e.tensor_copy(
                l0[:, 512:1024], d4[:]),
                reads=K1("d4"), writes=klb, cost=250)
            p.op("vector", lambda e, l0=l0: e.tensor_sub(
                l0[:, 0:512], l0[:, 0:512], l0[:, 512:1024]),
                reads=kla + klb, writes=kla, cost=250)
            p.op("vector", lambda e, l0=l0: e.scalar_tensor_tensor(
                l0[:, 512:1024], l0[:, 0:512], 0.5, l0[:, 512:1024],
                ALU.is_gt, ALU.add),
                reads=kla + klb, writes=klb, cost=250)
            p.op("vector", lambda e, l0=l0: e.scalar_tensor_tensor(
                l0[:, 512:1024], l0[:, 0:512], -0.5, l0[:, 512:1024],
                ALU.is_ge, ALU.add),
                reads=kla + klb, writes=klb, cost=250)
            p.op("vector", lambda e, l0=l0: e.tensor_scalar(
                l0[:, 512:1024], l0[:, 512:1024], 1.0, QMAX,
                ALU.subtract, ALU.min),
                reads=klb, writes=klb, cost=250)
            p.op("vector", lambda e, l0=l0, ic=ic: e.tensor_scalar(
                hqB[:, ic * 512:(ic + 1) * 512], l0[:, 512:1024],
                -QMAX - 1.0, sdb[:], ALU.max, ALU.mult),
                reads=klb + K1("sdb"), writes=[("hq", ic)], cost=250)

        for ic in range(NIC):
            decode_tile(0, ic)

        for hc in range(8):
            par = hc % 2
            no = 0
            nk = [("natd", 0)]
            p.op("sync", lambda e, hc=hc, no=no: e.dma_start(
                out=natd[:, no:no + II], in_=Wd[hc * 128:(hc + 1) * 128, :]),
                writes=nk, dma=True)
            wc = 12 + par
            rc = 8 + par
            nmc = 6 + par
            p.op("vector", lambda e, no=no, wc=wc: e.reduce_max(
                smv[:, wc:wc + 1], natd[:, no:no + II], AX.X,
                apply_absolute_value=True),
                reads=nk, writes=sc(wc))
            p.op("vector", lambda e, wc=wc, rc=rc: e.reciprocal(
                smv[:, rc:rc + 1], smv[:, wc:wc + 1]),
                reads=sc(wc), writes=sc(rc))
            p.op("vector", lambda e, wc=wc, nmc=nmc: e.tensor_scalar(
                smv[:, nmc:nmc + 1], smv[:, wc:wc + 1], -1.0, None,
                ALU.mult), reads=sc(wc), writes=sc(nmc))
            p.op("vector", lambda e, rc=rc: e.tensor_scalar(
                smv[:, 10:11], smv[:, rc:rc + 1], QMAX, None, ALU.mult),
                reads=sc(rc), writes=sc(10))
            p.op("vector", lambda e, wc=wc: e.reciprocal(
                smv[:, wc:wc + 1], smv[:, 10:11]),
                reads=sc(10), writes=sc(wc))
            p.defc = 650
            for hf in range(2):
                fo = no + hf * 1408
                p.op("vector", lambda e, fo=fo: e.tensor_scalar(
                    natd[:, fo:fo + 1408], natd[:, fo:fo + 1408], QMAX, None,
                    ALU.mult), reads=nk, writes=nk)
                p.op("vector", lambda e, fo=fo, rc=rc: e.tensor_scalar(
                    A5[:], natd[:, fo:fo + 1408], smv[:, rc:rc + 1], None,
                    ALU.mult), reads=nk + sc(rc), writes=K1("A5"))
                p.op("vector", lambda e, fo=fo, nmc=nmc: e.scalar_tensor_tensor(
                    A6[:], A5[:], smv[:, nmc:nmc + 1], natd[:, fo:fo + 1408],
                    ALU.mult, ALU.add),
                    reads=K1("A5") + nk + sc(nmc), writes=K1("A6"))
                p.op("vector", lambda e, rc=rc: e.scalar_tensor_tensor(
                    A5[:], A6[:], smv[:, rc:rc + 1], A5[:],
                    ALU.mult, ALU.add),
                    reads=K1("A6") + K1("A5") + sc(rc), writes=K1("A5"))
                p.op("vector", lambda e, fo=fo, nmc=nmc: e.scalar_tensor_tensor(
                    A6[:], A5[:], smv[:, nmc:nmc + 1], natd[:, fo:fo + 1408],
                    ALU.mult, ALU.add),
                    reads=K1("A5") + nk + sc(nmc), writes=K1("A6"))
                p.op("vector", lambda e, rc=rc: e.scalar_tensor_tensor(
                    A5[:], A6[:], smv[:, rc:rc + 1], A5[:],
                    ALU.mult, ALU.add),
                    reads=K1("A6") + K1("A5") + sc(rc), writes=K1("A5"))
                p.op("vector", lambda e: e.tensor_copy(A4[:], A5[:]),
                     reads=K1("A5"), writes=K1("A4"))
                p.op("vector", lambda e: e.tensor_copy(A6[:], A4[:]),
                     reads=K1("A4"), writes=K1("A6"))
                p.op("vector", lambda e, fo=fo: e.tensor_sub(
                    natd[:, fo:fo + 1408], A5[:], A6[:]),
                    reads=K1("A5") + K1("A6"), writes=nk)
                p.op("vector", lambda e, fo=fo: e.scalar_tensor_tensor(
                    A6[:], natd[:, fo:fo + 1408], 0.5, A6[:],
                    ALU.is_ge, ALU.add), reads=nk + K1("A6"), writes=K1("A6"))
                p.op("vector", lambda e, fo=fo: e.scalar_tensor_tensor(
                    A6[:], natd[:, fo:fo + 1408], -0.5, A6[:],
                    ALU.is_ge, ALU.add), reads=nk + K1("A6"), writes=K1("A6"))
                p.op("vector", lambda e, fo=fo: e.scalar_tensor_tensor(
                    natd[:, fo:fo + 1408], A6[:], -1.0, A5[:],
                    ALU.add, ALU.subtract),
                    reads=K1("A6") + K1("A5"), writes=nk)
                p.op("vector", lambda e, fo=fo: e.tensor_scalar(
                    natd[:, fo:fo + 1408], natd[:, fo:fo + 1408], 0.5, None,
                    ALU.is_ge), reads=nk, writes=nk)
                for hh in range(2):
                    ha = hh * 704
                    hb = (1 - hh) * 704
                    p.op("scalar", lambda e, ha=ha: e.activation(
                        A5[:, ha:ha + 704], A6[:, ha:ha + 704],
                        AF.Copy, bias=-0.5, scale=0.5),
                        reads=K1("A6"), writes=K1("A5"), cost=450)
                    p.op("vector", lambda e, ha=ha: e.tensor_copy(
                        wtp2[:], A5[:, ha:ha + 704]),
                        reads=K1("A5"), writes=K1("wtp2"), cost=350)
                    p.op("vector", lambda e, hb=hb: e.tensor_copy(
                        A5[:, hb:hb + 704], wtp2[:]),
                        reads=K1("wtp2"), writes=K1("A5"), cost=350)
                    p.op("vector", lambda e, ha=ha, hb=hb: e.tensor_sub(
                        A5[:, ha:ha + 704], A5[:, ha:ha + 704],
                        A5[:, hb:hb + 704]),
                        reads=K1("A5"), writes=K1("A5"), cost=350)
                    p.op("vector", lambda e, ha=ha, fo=fo: e.tensor_tensor(
                        A5[:, ha:ha + 704], A5[:, ha:ha + 704],
                        natd[:, fo + ha:fo + ha + 704], ALU.mult),
                        reads=K1("A5") + nk, writes=K1("A5"), cost=350)
                    p.op("vector", lambda e, ha=ha: e.tensor_tensor(
                        A5[:, ha:ha + 704], A5[:, ha:ha + 704],
                        A5[:, ha:ha + 704], ALU.mult),
                        reads=K1("A5"), writes=K1("A5"), cost=350)
                    p.op("vector", lambda e, ha=ha: e.scalar_tensor_tensor(
                        A6[:, ha:ha + 704], A5[:, ha:ha + 704], -4.0,
                        A6[:, ha:ha + 704], ALU.mult, ALU.add),
                        reads=K1("A5") + K1("A6"), writes=K1("A6"), cost=350)
                p.op("vector", lambda e, fo=fo, wc=wc: e.tensor_scalar(
                    natd[:, fo:fo + 1408], A6[:], 1.0,
                    smv[:, wc:wc + 1], ALU.subtract, ALU.mult),
                    reads=K1("A6") + sc(wc), writes=nk)
            p.defc = None
            for c in range(NIC):
                tp = (pT0, pT1, pT2, pT3)[c % 4]
                col = hc * II + c * 128
                p.op("tensor", lambda e, no=no, c=c, tp=tp: e.transpose(
                    tp[:, 0:128], natd[:, no + c * 128: no + (c + 1) * 128],
                    idt[:]),
                    reads=nk + K1("idt"), writes=psk(tp))
                p.op("scalar", lambda e, tp=tp, col=col: e.activation(
                    wdT[:, col:col + 128], tp[:, 0:128], AF.Copy, scale=1.0),
                    reads=psk(tp), writes=[("wdT", col // 128)], cost=250)

        for tb in range(4):
            if tb > 0:
                for ic in range(NIC):
                    decode_tile(tb, ic)
            for hc in range(8):
                ps = (pA, pB)[hc % 2]
                def dmf(e, hc=hc, ps=ps, lo=0, hi=NIC):
                    r = None
                    for c in range(lo, hi):
                        r = e.matmul(
                            ps[:],
                            wdT[:, hc * II + c * 128: hc * II + (c + 1) * 128],
                            hqB[:, c * 512:(c + 1) * 512],
                            start=(c == 0), stop=(c == NIC - 1))
                    return r
                rd_a = [("wdT", (hc * II + c * 128) // 128) for c in range(11)]
                rd_a += [("hq", c) for c in range(11)]
                rd_b = [("wdT", (hc * II + c * 128) // 128)
                        for c in range(11, NIC)]
                rd_b += [("hq", c) for c in range(11, NIC)]
                import functools
                p.op("tensor", functools.partial(dmf, lo=0, hi=11),
                     reads=rd_a, writes=psk(ps), cost=6400)
                p.op("tensor", functools.partial(dmf, lo=11, hi=NIC),
                     reads=rd_b, writes=psk(ps), cost=6400)
                p.op("scalar", lambda e, ps=ps, hc=hc: e.activation(
                    dF[:, hc * 512:(hc + 1) * 512], ps[:], AF.Copy,
                    scale=1.0),
                    reads=psk(ps), writes=[("dF", hc)], cost=600)
            for tt in range(4):
                tok0 = tb * 512 + tt * 128
                p.op("sync", lambda e, tok0=tok0: e.dma_start(
                    out=aT[:], in_=acc[tok0:tok0 + 128, :]),
                    writes=K1("aT"), dma=True)
                p.op("sync", lambda e, tok0=tok0: e.dma_start(
                    out=aT1[:], in_=acc1[tok0:tok0 + 128, :]),
                    writes=K1("aT1"), dma=True)
                p.op("vector", lambda e: e.tensor_scalar(
                    aT2[:], aT[:], G0R, G0R, ALU.mult, ALU.subtract),
                    reads=K1("aT"), writes=K1("aT2"))
                p.op("vector", lambda e: e.scalar_tensor_tensor(
                    aT2[:], aT1[:], G1R, aT2[:], ALU.mult, ALU.add),
                    reads=K1("aT1") + K1("aT2"), writes=K1("aT2"))
                for hc in range(8):
                    tp = (pT0, pT1, pT2, pT3)[hc % 4]
                    p.op("tensor", lambda e, hc=hc, tt=tt, tp=tp: e.transpose(
                        tp[:, 0:128], dF[:, hc * 512 + tt * 128:
                                         hc * 512 + (tt + 1) * 128], idt[:]),
                        reads=[("dF", hc)] + K1("idt"), writes=psk(tp))
                    p.op("scalar", lambda e, tp=tp, hc=hc: e.activation(
                        oT[:, hc * 128:(hc + 1) * 128], tp[:, 0:128],
                        AF.Copy, scale=1.0),
                        reads=psk(tp), writes=[("oT", hc)], cost=250)
                p.op("vector", lambda e: e.tensor_tensor(
                    oT[:], oT[:], aT2[:], ALU.add),
                    reads=[("oT", hc) for hc in range(8)] + K1("aT2"),
                    writes=[("oT", hc) for hc in range(8)])
                p.op("gpsimd", lambda e, tok0=tok0: e.dma_start(
                    out=out_o[tok0:tok0 + 128, :], in_=oT[:]),
                    reads=[("oT", hc) for hc in range(8)], dma=True)

        p.emit_into(block, sems, dsems)
    return nc



def _tile_of_slot(parity, j):
    if parity == 0:
        return 2 * j if j < 8 else 2 * j + 1
    return 2 * j + 1 if j < 8 else 2 * j


def _masks(parity):
    tri = (np.arange(128)[None, :] <= np.arange(128)[:, None]).astype(np.float32)
    m_tri0 = np.concatenate([tri, np.zeros((128, 128), np.float32)], axis=1)
    m_1tri = np.concatenate([np.ones((128, 128), np.float32), tri], axis=1)
    if parity == 0:
        return np.ascontiguousarray(np.concatenate([m_tri0, m_1tri], axis=0))
    return np.ascontiguousarray(np.concatenate([m_1tri, m_tri0], axis=0))


def kernel(L0_attn_X, L0_attn_Wq, L0_attn_bq, L0_attn_Wk, L0_attn_bk,
           L0_attn_Wv, L0_attn_bv, L0_ffn_clip_x, L0_ffn_clip_down,
           L0_ffn_w_gate, L0_ffn_w_up, L0_ffn_w_down):
    X = np.ascontiguousarray(np.asarray(L0_attn_X, dtype=np.float32))
    Wq = np.ascontiguousarray(np.asarray(L0_attn_Wq, np.float32))
    Wk = np.ascontiguousarray(np.asarray(L0_attn_Wk, np.float32))
    Wv = np.ascontiguousarray(np.asarray(L0_attn_Wv, np.float32))
    bq = np.ascontiguousarray(np.asarray(L0_attn_bq, np.float32).reshape(8, 128).T)
    bk = np.ascontiguousarray(np.asarray(L0_attn_bk, np.float32).reshape(8, 128).T)
    bv = np.ascontiguousarray(np.asarray(L0_attn_bv, np.float32).reshape(8, 128).T)
    Wg = np.ascontiguousarray(np.asarray(L0_ffn_w_gate, np.float32))
    Wu = np.ascontiguousarray(np.asarray(L0_ffn_w_up, np.float32))
    Wd = np.ascontiguousarray(np.asarray(L0_ffn_w_down, np.float32))
    cx = np.ascontiguousarray(np.asarray(L0_ffn_clip_x, np.float32).reshape(1, 1))
    cd = np.ascontiguousarray(np.asarray(L0_ffn_clip_down, np.float32).reshape(1, 1))
    ident = np.eye(128, dtype=np.float32)

    nc1 = build_l1()
    in1 = []
    for c in range(8):
        b = c // 2
        par = c % 2
        others = [X[bb] for bb in range(B) if bb != b]
        xstat = np.concatenate([X[b]] + others, axis=0)
        own = np.concatenate(
            [X[b, _tile_of_slot(par, j) * 128:(_tile_of_slot(par, j) + 1) * 128, :]
             for j in range(NS)], axis=0)
        in1.append(dict(Xstat=np.ascontiguousarray(xstat),
                        Xown=np.ascontiguousarray(own),
                        Wq=Wq, Wk=Wk, Wv=Wv, bqv=bq, bkv=bk, bvv=bv,
                        Wg=Wg, Wu=Wu, clipx=cx, masks=_masks(par), ident=ident))
    r1 = run_bass_kernel_spmd(nc1, in1, core_ids=list(range(8)))

    hm8 = np.ascontiguousarray(np.stack(
        [np.asarray(r1.results[c]["hm"], np.float32)[0] for c in range(8)],
        axis=0))
    nc2 = build_l2()
    in2 = []
    for c in range(8):
        in2.append(dict(
            hi0=np.ascontiguousarray(np.asarray(r1.results[c]["hi0"], np.float32)),
            hi1=np.ascontiguousarray(np.asarray(r1.results[c]["hi1"], np.float32)),
            acc=np.ascontiguousarray(np.asarray(r1.results[c]["acc"], np.float32)),
            acc1=np.ascontiguousarray(np.asarray(r1.results[c]["acc1"], np.float32)),
            hm8=hm8, Wd=Wd, clipd=cd, ident=ident))
    r2 = run_bass_kernel_spmd(nc2, in2, core_ids=list(range(8)))

    out = np.empty((B, S, D), np.float32)
    for c in range(8):
        b = c // 2
        par = c % 2
        o = np.asarray(r2.results[c]["out"], np.float32)
        for j in range(NS):
            t = _tile_of_slot(par, j)
            out[b, t * 128:(t + 1) * 128, :] = o[j * 128:(j + 1) * 128, :]
    return out
